# version 01 (final)
# baseline (speedup 1.0000x reference)
import numpy as np
import concourse.bass as bass
import concourse.mybir as mybir
from concourse.bass_utils import run_bass_kernel_spmd

F32 = mybir.dt.float32
BF16 = mybir.dt.bfloat16
AF = mybir.ActivationFunctionType
ALU = mybir.AluOpType
AX = mybir.AxisListType

D = 2048
TL = 2048
NKC = D // 128
IF = 8192
TP = 1024
NTP = TL // TP
EPS = 1e-5
NCC = 8

FFN_CHUNK_ORDER = [16 * r + gc for ip in range(2)
                   for gc in range(8 * ip, 8 * ip + 8) for r in range(4)]


class Sem:
    def __init__(self, nc, name):
        self.h = nc.alloc_semaphore(name)
        self.v = 0


class K:
    ENGINES = ("sync", "vector", "scalar", "tensor", "gpsimd")

    def __init__(self):
        self.nc = bass.Bass("TRN2", target_bir_lowering=False, debug=False)
        self.ops = {e: [] for e in self.ENGINES}
        self.sems = {}
        self.meta = {}
        self.sb = {}

    def sem(self, name):
        if name not in self.sems:
            self.sems[name] = Sem(self.nc, name)
        return self.sems[name]

    def on(self, eng, fn):
        self.ops[eng].append(fn)

    def wait(self, eng, sem, v=None):
        vv = sem.v if v is None else v
        if vv <= 0:
            return
        self.on(eng, lambda e, s=sem.h, t=vv: e.wait_ge(s, t))

    def buf_guard(self, name, eng, nbuf=2):
        uses = self.meta.setdefault(name, [])
        if len(uses) >= nbuf:
            sname, v = uses[-nbuf]
            self.wait(eng, self.sem(sname), v)

    def buf_release(self, name, sname):
        self.meta.setdefault(name, []).append((sname, self.sem(sname).v))

    def do(self, eng, fn, inc=None, n=1):
        if inc is not None:
            inc.v += n
            self.on(eng, lambda e, f=fn, s=inc.h, m=n: f(e).then_inc(s, m))
        else:
            self.on(eng, fn)

    GO_BIG = 1 << 30

    def finalize(self):
        go = self.sem("s_go")
        for s in self.sems.values():
            self.nc.gpsimd.sem_clear(s.h)
        self.nc.gpsimd.sem_inc(go.h, self.GO_BIG)
        for e in self.ENGINES:
            if e != "gpsimd":
                self.ops[e].insert(0, lambda eng, g=go.h, b=self.GO_BIG:
                                   eng.wait_ge(g, b))
        block = self.nc.Block()
        with block:
            for e in self.ENGINES:
                fns = self.ops[e]
                getattr(block, e)(lambda eng, fns=fns: [fn(eng) for fn in fns])
        return self.nc


def alloc_common(k):
    nc = k.nc
    sb = k.sb
    sb["ACT"] = nc.alloc_sbuf_tensor("ACT", [128, 32, TP], BF16)
    sb["XLN"] = nc.alloc_sbuf_tensor("XLN", [128, NKC, TP], BF16)
    sb["WB"] = nc.alloc_sbuf_tensor("WB", [128, 2, 4096], BF16)
    sb["SCR32"] = nc.alloc_sbuf_tensor("SCR32", [128, 8, TP], F32)
    sb["SGF"] = nc.alloc_sbuf_tensor("SGF", [128, 2, TP], F32)
    sb["XFF"] = nc.alloc_sbuf_tensor("XFF", [128, 2, TP], F32)
    sb["EPI"] = nc.alloc_sbuf_tensor("EPI", [128, 2, TP], F32)
    sb["XBF"] = nc.alloc_sbuf_tensor("XBF", [128, 2, TP], BF16)
    sb["APL"] = nc.alloc_sbuf_tensor("APL", [128, 2, TP], F32)
    sb["SQB"] = nc.alloc_sbuf_tensor("SQB", [128, 2, TP], BF16)

    sb["EXT"] = nc.alloc_sbuf_tensor("EXT", [128, 8, 8], F32)
    sb["NMX"] = nc.alloc_sbuf_tensor("NMX", [128, 8], F32)
    sb["RNK"] = nc.alloc_sbuf_tensor("RNK", [128, 8, 8], F32)
    sb["RT1"] = nc.alloc_sbuf_tensor("RT1", [128, 8, 8], F32)
    sb["DEN"] = nc.alloc_sbuf_tensor("DEN", [128, 8, 2], F32)
    sb["CWT"] = nc.alloc_sbuf_tensor("CWT", [128, 8, 8], F32)
    sb["CWF"] = nc.alloc_sbuf_tensor("CWF", [8, TP], F32)
    sb["RWS"] = nc.alloc_sbuf_tensor("RWS", [128, NKC, 8], BF16)
    sb["IDENTF"] = nc.alloc_sbuf_tensor("IDENTF", [128, 128], F32)
    sb["ONESB"] = nc.alloc_sbuf_tensor("ONESB", [128, 1], BF16)
    sb["ONESRF"] = nc.alloc_sbuf_tensor("ONESRF", [1, 128], F32)
    sb["CF"] = nc.alloc_sbuf_tensor("CF", [128, 4], F32)
    sb["PS"] = nc.alloc_psum_tensor("PS", [128, 8, 512], F32)
    s_init = k.sem("s_init")
    k.do("gpsimd", lambda e: e.memset(sb["ONESB"][:], 1.0), inc=s_init)
    k.do("gpsimd", lambda e: e.memset(sb["ONESRF"][:], 1.0), inc=s_init)
    k.do("gpsimd", lambda e: e.memset(sb["CF"][:, 0:1], EPS), inc=s_init)
    k.do("gpsimd", lambda e: e.memset(sb["CF"][:, 1:2], 1.0), inc=s_init)
    s_initd = k.sem("s_initd")
    ident = nc.inline_tensor(np.eye(128, dtype=np.float32), "identf")
    k.do("sync", lambda e: e.dma_start(out=sb["IDENTF"][:], in_=ident.ap()),
         inc=s_initd, n=16)
    selv = np.zeros((8, 8 * 128), dtype=np.float32)
    for e8 in range(8):
        selv[e8, e8 * 128:(e8 + 1) * 128] = 1.0
    sel = nc.inline_tensor(selv, "selv")
    sb["SELS"] = nc.alloc_sbuf_tensor("SELS", [8, 8 * 128], F32)
    k.do("sync", lambda e: e.dma_start(out=sb["SELS"][:], in_=sel.ap()),
         inc=s_initd, n=16)


def scrb(k, i):
    return k.sb["SCR32"][:].rearrange("p a b -> p (a b)").bitcast(BF16)[
        :, i * TP:(i + 1) * TP]


def cwb_view(k, e):
    return k.sb["SCR32"][:, e]


def ps_bank(k, b0, nb=1, cols=None):
    PS = k.sb["PS"]
    v = PS[:, b0:b0 + nb, :]
    v = v.rearrange("p a b -> p (a b)")
    if cols is not None:
        v = v[:, cols[0]:cols[1]]
    return v


def emit_ln(k, x_in, t0, gamma_beta=None, f32_out=None, write_xln=True):
    sb = k.sb
    s_x = [k.sem("ln_x0"), k.sem("ln_x1")]
    s_cast = k.sem("ln_cast")
    s_castfree = k.sem("ln_castfree")
    s_xfree = k.sem("ln_xfree")
    s_stat = k.sem("ln_stat")
    s_row = k.sem("ln_row")
    s_sqrt = k.sem("ln_sqrt")
    s_ab = k.sem("ln_ab")
    s_bc = k.sem("ln_bc")
    s_xln = k.sem("ln_xln")
    ROWS = sb["SCR32"]
    XFF, XBF, SQB, XLN = sb["XFF"], sb["XBF"], sb["SQB"], sb["XLN"]

    for kc in range(NKC):
        j = kc % 2
        k.buf_guard("XFF", "sync")
        k.do("sync", lambda e, kc=kc, j=j: e.dma_start(
            out=XFF[:, j], in_=x_in[kc * 128:(kc + 1) * 128, t0:t0 + TP]),
            inc=s_x[j], n=16)
        k.wait("scalar", s_x[j])
        k.buf_guard("XBQ", "scalar")
        k.do("scalar", lambda e, j=j: e.copy(out=XBF[:, j], in_=XFF[:, j]))
        k.do("scalar", lambda e, j=j: e.square(out=SQB[:, j], in_=XFF[:, j]),
             inc=s_cast)
        k.buf_release("XFF", "ln_cast")
        if kc == 0:
            k.wait("tensor", k.sem("s_init"), 4)
            k.wait("tensor", k.sem("ffn_hb"), k.sem("ffn_hb").v)
            k.wait("tensor", s_xln, s_xln.v)
            k.wait("tensor", k.sem("ln_f32"), k.sem("ln_f32").v)
        k.wait("tensor", s_cast)
        for nck in range(TP // 512):
            k.do("tensor", lambda e, j=j, kc=kc, nck=nck: e.matmul(
                out=ps_bank(k, nck)[0:1, :], lhsT=sb["ONESB"][:],
                rhs=XBF[:, j, nck * 512:(nck + 1) * 512],
                start=(kc == 0), stop=(kc == NKC - 1)))
            k.do("tensor", lambda e, j=j, kc=kc, nck=nck: e.matmul(
                out=ps_bank(k, 2 + nck)[0:1, :], lhsT=sb["ONESB"][:],
                rhs=SQB[:, j, nck * 512:(nck + 1) * 512],
                start=(kc == 0), stop=(kc == NKC - 1)),
                inc=(s_castfree if nck == TP // 512 - 1 else None))
        k.buf_release("XBQ", "ln_castfree")
    rowm = ROWS[0:1, 0, :]
    rowq = ROWS[0:1, 1, :]
    rowv = ROWS[0:1, 2, :]
    rowsd = ROWS[0:1, 3, :]
    rowa = ROWS[0:1, 4, :]
    rowb = ROWS[0:1, 5, :]
    k.wait("vector", s_castfree, s_castfree.v)
    k.do("vector", lambda e: e.tensor_scalar_mul(
        out=rowm, in0=ps_bank(k, 0, 2)[0:1, :], scalar1=1.0 / D))
    k.do("vector", lambda e: e.tensor_scalar_mul(
        out=rowq, in0=ps_bank(k, 2, 2)[0:1, :], scalar1=1.0 / D))
    k.do("vector", lambda e: e.tensor_mul(out=rowv, in0=rowm, in1=rowm))
    k.do("vector", lambda e: e.tensor_sub(out=rowv, in0=rowq, in1=rowv),
         inc=s_row)
    k.wait("scalar", s_row)
    k.do("scalar", lambda e: e.activation(out=rowsd, in_=rowv, func=AF.Sqrt,
                                          bias=sb["CF"][0:1, 0:1], scale=1.0),
         inc=s_sqrt)
    k.wait("vector", s_sqrt)
    k.do("vector", lambda e: e.reciprocal(out=rowa, in_=rowsd))
    k.do("vector", lambda e: e.scalar_tensor_tensor(
        out=rowb, in0=rowm, scalar=-1.0, in1=rowa,
        op0=ALU.mult, op1=ALU.mult), inc=s_ab)
    k.wait("tensor", s_ab)
    for nck in range(TP // 512):
        k.do("tensor", lambda e, nck=nck: e.matmul(
            out=ps_bank(k, nck), lhsT=sb["ONESRF"][:],
            rhs=rowa[:, nck * 512:(nck + 1) * 512], start=True, stop=True))
        k.do("tensor", lambda e, nck=nck: e.matmul(
            out=ps_bank(k, 2 + nck), lhsT=sb["ONESRF"][:],
            rhs=rowb[:, nck * 512:(nck + 1) * 512], start=True, stop=True),
            inc=(s_bc if nck == TP // 512 - 1 else None))
    psA = ps_bank(k, 0, 2)
    psB = ps_bank(k, 2, 2)
    for kc in range(NKC):
        j = kc % 2
        k.buf_guard("XFF", "sync")
        k.do("sync", lambda e, kc=kc, j=j: e.dma_start(
            out=XFF[:, j], in_=x_in[kc * 128:(kc + 1) * 128, t0:t0 + TP]),
            inc=s_x[j], n=16)
        k.wait("vector", s_x[j])
        k.wait("vector", s_bc)
        if f32_out is not None:
            k.wait("vector", k.sem("ffn_out"), k.sem("ffn_out").v)
        k.do("vector", lambda e, j=j: e.tensor_mul(
            out=sb["EPI"][:, j], in0=XFF[:, j], in1=psA))
        if gamma_beta is None and f32_out is None:
            k.do("vector", lambda e, j=j, kc=kc: e.tensor_add(
                out=XLN[:, kc], in0=sb["EPI"][:, j], in1=psB), inc=s_xln)
        else:
            s_lnf = k.sem("ln_f32")
            k.do("vector", lambda e, j=j: e.tensor_add(
                out=sb["APL"][:, j], in0=sb["EPI"][:, j], in1=psB),
                inc=(s_lnf if f32_out is not None else None))
            if f32_out is not None:
                k.wait("sync", s_lnf)
                k.do("sync", lambda e, j=j, kc=kc, t0=t0: e.dma_start(
                    out=f32_out[kc * 128:(kc + 1) * 128, t0:t0 + TP],
                    in_=sb["APL"][:, j]), inc=k.sem("ffn_out"), n=16)
            if write_xln:
                if gamma_beta is not None:
                    k.do("vector", lambda e, j=j, kc=kc, gb=gamma_beta:
                         e.tensor_scalar_mul(
                             out=sb["EPI"][:, j], in0=sb["APL"][:, j],
                             scalar1=gb[:, kc, 0:1]))
                    k.do("vector", lambda e, j=j, kc=kc, gb=gamma_beta:
                         e.tensor_scalar_add(
                             out=XLN[:, kc], in0=sb["EPI"][:, j],
                             scalar1=gb[:, kc, 1:2]), inc=s_xln)
                else:
                    k.do("vector", lambda e, j=j, kc=kc: e.tensor_copy(
                        out=XLN[:, kc], in_=sb["APL"][:, j]), inc=s_xln)
        if gamma_beta is None and f32_out is None:
            k.buf_release("XFF", "ln_xln")
        elif write_xln:
            k.buf_release("XFF", "ln_xln")
        else:
            k.buf_release("XFF", "ln_f32")
    return s_xln


def emit_router(k, t0, rw_loaded):
    sb = k.sb
    EXT, NMX, RNK, RT1, DEN, CWT, CWF = (
        sb["EXT"], sb["NMX"], sb["RNK"], sb["RT1"], sb["DEN"],
        sb["CWT"], sb["CWF"])
    PS = sb["PS"]
    s_rsc = k.sem("rt_rsc")
    s_nm = k.sem("rt_nm")
    s_et = k.sem("rt_et")
    s_cw = k.sem("rt_cw")
    s_tr = k.sem("rt_tr")
    s_cwf = k.sem("rt_cwf")
    s_cwm = k.sem("rt_cwm")
    s_cwb = k.sem("rt_cwb")
    s_xln = k.sem("ln_xln")
    k.wait("tensor", s_xln)
    k.wait("tensor", k.sem("s_initd"), k.sem("s_initd").v)
    for nm in ("ffn_sg", "ffn_hb", "ffn_ep", "rt_et", "rt_cwf", "rt_cwb"):
        k.wait("tensor", k.sem(nm), k.sem(nm).v)
    k.wait("tensor", rw_loaded, rw_loaded.v)
    for tt in range(8):
        for kc in range(NKC):
            k.do("tensor", lambda e, kc=kc, tt=tt: e.matmul(
                out=PS[:, 4, tt * 8:(tt + 1) * 8],
                lhsT=k.sb["XLN"][:, kc, tt * 128:(tt + 1) * 128],
                rhs=k.sb["RWS"][:, kc, :],
                start=(kc == 0), stop=(kc == NKC - 1)),
                inc=(s_rsc if kc == NKC - 1 else None))
    rsc_all = s_rsc.v
    k.wait("vector", s_rsc, rsc_all)
    for tt in range(8):
        k.do("vector", lambda e, tt=tt: e.reduce_max(
            out=NMX[:, tt:tt + 1], in_=PS[:, 4, tt * 8:(tt + 1) * 8],
            axis=AX.X, negate=True), inc=s_nm)
    k.wait("scalar", s_nm)
    for tt in range(8):
        k.do("scalar", lambda e, tt=tt: e.activation(
            out=EXT[:, tt], in_=PS[:, 4, tt * 8:(tt + 1) * 8], func=AF.Exp,
            bias=NMX[:, tt:tt + 1], scale=1.0), inc=s_et)
    k.wait("vector", s_et)
    for tt in range(8):
        k.do("vector", lambda e, tt=tt: e.tensor_single_scalar(
            out=RNK[:, tt], in_=EXT[:, tt], scalar=EXT[:, tt, 0:1],
            op=ALU.is_lt))
    for jj in range(1, 8):
        for tt in range(8):
            k.do("vector", lambda e, tt=tt, jj=jj: e.tensor_single_scalar(
                out=RT1[:, tt], in_=EXT[:, tt], scalar=EXT[:, tt, jj:jj + 1],
                op=ALU.is_lt))
        for tt in range(8):
            k.do("vector", lambda e, tt=tt: e.tensor_add(
                out=RNK[:, tt], in0=RNK[:, tt], in1=RT1[:, tt]))
    for jj in range(0, 7):
        for tt in range(8):
            k.do("vector", lambda e, tt=tt, jj=jj: e.tensor_single_scalar(
                out=RT1[:, tt], in_=EXT[:, tt], scalar=EXT[:, tt, jj:jj + 1],
                op=ALU.is_equal))
        for tt in range(8):
            k.do("vector", lambda e, tt=tt, jj=jj: e.tensor_add(
                out=RNK[:, tt, jj + 1:8], in0=RNK[:, tt, jj + 1:8],
                in1=RT1[:, tt, jj + 1:8]))
    for tt in range(8):
        k.do("vector", lambda e, tt=tt: e.tensor_single_scalar(
            out=RT1[:, tt], in_=RNK[:, tt], scalar=1.5, op=ALU.is_lt))
    for tt in range(8):
        k.do("vector", lambda e, tt=tt: e.tensor_mul(
            out=RT1[:, tt], in0=EXT[:, tt], in1=RT1[:, tt]))
    for tt in range(8):
        k.do("vector", lambda e, tt=tt: e.reduce_sum(
            out=DEN[:, tt, 0:1], in_=RT1[:, tt], axis=AX.X))
    for tt in range(8):
        k.do("vector", lambda e, tt=tt: e.reciprocal(
            out=DEN[:, tt, 1:2], in_=DEN[:, tt, 0:1]))
    for tt in range(8):
        k.do("vector", lambda e, tt=tt: e.tensor_scalar_mul(
            out=CWT[:, tt], in0=RT1[:, tt], scalar1=DEN[:, tt, 1:2]),
            inc=s_cw)
    cw_all = s_cw.v
    for tt in range(8):
        k.wait("tensor", s_cw, cw_all)
        k.wait("tensor", s_cwf)
        k.do("tensor", lambda e, tt=tt: e.transpose(
            out=PS[0:8, 5, 0:128], in_=CWT[:, tt],
            identity=k.sb["IDENTF"][:]), inc=s_tr)
        k.wait("vector", s_tr)
        k.do("vector", lambda e, tt=tt: e.tensor_copy(
            out=CWF[0:8, tt * 128:(tt + 1) * 128],
            in_=PS[0:8, 5, 0:128]), inc=s_cwf)
    for e2 in range(8):
        k.wait("tensor", s_cwb)
        k.wait("tensor", s_cwf, s_cwf.v)
        for nck in range(TP // 512):
            k.do("tensor", lambda e, e2=e2, nck=nck: e.matmul(
                out=ps_bank(k, 6 + nck),
                lhsT=k.sb["SELS"][:, e2 * 128:(e2 + 1) * 128],
                rhs=CWF[:, nck * 512:(nck + 1) * 512],
                start=True, stop=True),
                inc=(s_cwm if nck == TP // 512 - 1 else None))
        k.wait("vector", s_cwm)
        k.do("vector", lambda e, e2=e2: e.tensor_copy(
            out=cwb_view(k, e2), in_=ps_bank(k, 6, 2)), inc=s_cwb)
    return s_cwb


def emit_mlp(k, x_in, x_out, wg_t, wu_t, wd_t, part_dram, kind="ffn24",
             rw_t=None, scale=0.5):
    sb = k.sb
    ACT, XLN, WB, SGF, XFF, EPI = (sb["ACT"], sb["XLN"], sb["WB"],
                                   sb["SGF"], sb["XFF"], sb["EPI"])
    s_w = [k.sem("ffn_w0"), k.sem("ffn_w1")]
    s_wd2 = [k.sem("ffn_wd0"), k.sem("ffn_wd1")]
    s_epin2 = [k.sem("ffn_epin0"), k.sem("ffn_epin1")]
    s_mmg = k.sem("ffn_mmg")
    s_mmu = k.sem("ffn_mmu")
    s_sg = k.sem("ffn_sg")
    s_hb = k.sem("ffn_hb")
    s_spr = k.sem("ffn_spr")
    s_wd = k.sem("ffn_wd")
    s_pd = k.sem("ffn_pd")
    s_ep = k.sem("ffn_ep")
    s_epin = k.sem("ffn_epin")
    s_out = k.sem("ffn_out")

    s_rw = k.sem("mlp_rw")
    if kind == "moe":
        k.do("gpsimd", lambda e: e.dma_start(
            out=k.sb["RWS"][:], in_=rw_t.rearrange("a p e -> p a e")),
            inc=s_rw, n=16)
    for pp in range(NTP):
        t0 = pp * TP
        s_xln = emit_ln(k, x_in, t0)
        xln_done = s_xln.v
        if kind == "moe":
            emit_router(k, t0, s_rw)
        for ip in range(2):
            hb_base = s_hb.v
            for s in range(32):
                if kind == "ffn24":
                    c = FFN_CHUNK_ORDER[ip * 32 + s]
                else:
                    c = ip * 32 + s
                j = s % 2
                k.wait("gpsimd", s_mmu, s_mmu.v - 1)
                if s <= 1:
                    k.wait("gpsimd", s_pd, s_pd.v)
                k.do("gpsimd", lambda e, c=c, j=j: e.dma_start(
                    out=WB[:, j, 0:2048], in_=wg_t[c]), inc=s_w[j], n=16)
                k.do("gpsimd", lambda e, c=c, j=j: e.dma_start(
                    out=WB[:, j, 2048:4096], in_=wu_t[c]), inc=s_w[j], n=16)
                k.wait("tensor", s_w[j])
                if s == 0:
                    k.wait("tensor", s_xln, xln_done)
                    k.wait("tensor", s_ep, s_ep.v)
                    if kind == "moe":
                        for nm in ("rt_et", "rt_cwf", "rt_cwb"):
                            k.wait("tensor", k.sem(nm), k.sem(nm).v)
                if s == 1:
                    k.wait("tensor", s_xln, xln_done)
                k.wait("tensor", s_sg, s_sg.v - 1)
                k.wait("tensor", s_hb, s_hb.v - 1)
                gb = 4 if j == 0 else 0
                for kc in range(NKC):
                    for nck in range(TP // 512):
                        k.do("tensor", lambda e, j=j, kc=kc, nck=nck, gb=gb: e.matmul(
                            out=ps_bank(k, gb + nck),
                            lhsT=WB[:, j, kc * 128:(kc + 1) * 128],
                            rhs=XLN[:, kc, nck * 512:(nck + 1) * 512],
                            start=(kc == 0), stop=(kc == NKC - 1)),
                            inc=(s_mmg if kc == NKC - 1 and nck == TP // 512 - 1
                                 else None))
                for kc in range(NKC):
                    for nck in range(TP // 512):
                        k.do("tensor", lambda e, j=j, kc=kc, nck=nck, gb=gb: e.matmul(
                            out=ps_bank(k, gb + 2 + nck),
                            lhsT=WB[:, j, 2048 + kc * 128:2048 + (kc + 1) * 128],
                            rhs=XLN[:, kc, nck * 512:(nck + 1) * 512],
                            start=(kc == 0), stop=(kc == NKC - 1)),
                            inc=(s_mmu if kc == NKC - 1 and nck == TP // 512 - 1
                                 else None))
                if s <= 1:
                    k.wait("scalar", s_out, s_out.v)
                k.wait("scalar", s_mmg)
                k.do("scalar", lambda e, j=j, gb=gb: e.activation(
                    out=SGF[:, j], in_=ps_bank(k, gb, 2), func=AF.Silu),
                    inc=s_sg)
                k.wait("vector", s_sg)
                k.wait("vector", s_mmu)
                sl = s
                if kind == "moe":
                    ex = (ip * 32 + s) // 8
                    if s <= 1:
                        k.wait("vector", k.sem("ffn_epin0"), k.sem("ffn_epin0").v)
                        k.wait("vector", k.sem("ffn_epin1"), k.sem("ffn_epin1").v)
                    k.do("vector", lambda e, j=j, gb=gb: e.tensor_mul(
                        out=EPI[:, j], in0=SGF[:, j], in1=ps_bank(k, gb + 2, 2)))
                    k.do("vector", lambda e, j=j, sl=sl, ex=ex: e.tensor_mul(
                        out=ACT[:, sl], in0=EPI[:, j], in1=cwb_view(k, ex)),
                        inc=s_hb)
                else:
                    k.do("vector", lambda e, j=j, sl=sl, gb=gb: e.tensor_mul(
                        out=ACT[:, sl], in0=SGF[:, j], in1=ps_bank(k, gb + 2, 2)),
                        inc=s_hb)
                if kind == "ffn24" and s % 4 == 3:
                    base = sl - 3
                    V = "vector"
                    s_abs = k.sem("ffn_abs")

                    def T(fn):
                        k.do(V, fn)
                    k.wait("scalar", s_hb)
                    for r in range(4):
                        k.do("scalar", lambda e, r=r, base=base: e.activation(
                            out=scrb(k, r), in_=ACT[:, base + r], func=AF.Abs),
                            inc=(s_abs if r == 3 else None))
                    k.wait(V, s_abs)
                    pairs = [(0, 1, 4), (0, 2, 5), (0, 3, 6),
                             (1, 2, 7), (1, 3, 8), (2, 3, 9)]
                    for i, jj, dst in pairs:
                        k.do(V, lambda e, i=i, jj=jj, dst=dst: e.tensor_tensor(
                            out=scrb(k, dst), in0=scrb(k, i), in1=scrb(k, jj),
                            op=ALU.is_ge))
                    T(lambda e: e.tensor_add(out=scrb(k, 0), in0=scrb(k, 4), in1=scrb(k, 5)))
                    T(lambda e: e.tensor_add(out=scrb(k, 0), in0=scrb(k, 0), in1=scrb(k, 6)))
                    T(lambda e: e.tensor_add(out=scrb(k, 1), in0=scrb(k, 7), in1=scrb(k, 8)))
                    T(lambda e: e.tensor_sub(out=scrb(k, 1), in0=scrb(k, 4), in1=scrb(k, 1)))
                    T(lambda e: e.tensor_add(out=scrb(k, 2), in0=scrb(k, 5), in1=scrb(k, 7)))
                    T(lambda e: e.tensor_sub(out=scrb(k, 2), in0=scrb(k, 2), in1=scrb(k, 9)))
                    T(lambda e: e.tensor_add(out=scrb(k, 3), in0=scrb(k, 6), in1=scrb(k, 8)))
                    T(lambda e: e.tensor_add(out=scrb(k, 3), in0=scrb(k, 3), in1=scrb(k, 9)))
                    T(lambda e: e.tensor_single_scalar(out=scrb(k, 4), in_=scrb(k, 0), scalar=1.5, op=ALU.is_ge))
                    T(lambda e: e.tensor_single_scalar(out=scrb(k, 5), in_=scrb(k, 1), scalar=-0.5, op=ALU.is_lt))
                    T(lambda e: e.tensor_single_scalar(out=scrb(k, 6), in_=scrb(k, 2), scalar=0.5, op=ALU.is_lt))
                    T(lambda e: e.tensor_single_scalar(out=scrb(k, 7), in_=scrb(k, 3), scalar=1.5, op=ALU.is_lt))
                    for r in range(4):
                        k.do(V, lambda e, r=r, base=base: e.tensor_tensor(
                            out=ACT[:, base + r], in0=ACT[:, base + r],
                            in1=scrb(k, 4 + r), op=ALU.mult),
                            inc=(s_spr if r == 3 else None))
            spr_done = s_spr.v
            for dc in range(NKC):
                j = dc % 2
                k.wait("gpsimd", s_pd, s_pd.v - 1)
                if dc <= 1:
                    k.wait("gpsimd", s_mmu, s_mmu.v)
                k.do("gpsimd", lambda e, dc=dc, ip=ip, j=j: e.dma_start(
                    out=WB[:, j, 0:4096],
                    in_=wd_t[dc][:, ip * 4096:(ip + 1) * 4096]), inc=s_wd2[j], n=16)
                k.wait("tensor", s_wd2[j])
                if dc == 0:
                    if kind == "ffn24":
                        k.wait("tensor", s_spr, spr_done)
                    else:
                        k.wait("tensor", s_hb, s_hb.v)
                k.wait("tensor", s_ep, s_ep.v - 1)
                db = 4 if j == 0 else 6
                for il in range(32):
                    for nck in range(TP // 512):
                        k.do("tensor", lambda e, j=j, il=il, nck=nck, db=db: e.matmul(
                            out=ps_bank(k, db + nck),
                            lhsT=WB[:, j, il * 128:(il + 1) * 128],
                            rhs=ACT[:, il, nck * 512:(nck + 1) * 512],
                            start=(il == 0), stop=(il == 31)),
                            inc=(s_pd if il == 31 and nck == TP // 512 - 1
                                 else None))
                if ip == 0:
                    if dc <= 1:
                        k.wait("vector", s_out, s_out.v)
                    k.wait("vector", s_pd)
                    k.do("vector", lambda e, j=j, db=db: e.tensor_copy(
                        out=SGF[:, j], in_=ps_bank(k, db, 2)), inc=s_ep)
                    k.wait("sync", s_ep)
                    k.do("sync", lambda e, dc=dc, j=j: e.dma_start(
                        out=part_dram[dc * 128:(dc + 1) * 128, :],
                        in_=SGF[:, j]), inc=s_out, n=16)
                else:
                    k.wait("sync", s_ep, s_ep.v - 1)
                    if dc <= 1:
                        k.wait("sync", s_hb, s_hb.v)
                    k.do("sync", lambda e, dc=dc, j=j: e.dma_start(
                        out=EPI[:, j], in_=part_dram[dc * 128:(dc + 1) * 128, :]),
                        inc=s_epin2[j], n=16)
                    k.buf_guard("XFF", "sync")
                    k.do("sync", lambda e, dc=dc, j=j, t0=t0: e.dma_start(
                        out=XFF[:, j], in_=x_in[dc * 128:(dc + 1) * 128, t0:t0 + TP]),
                        inc=s_epin2[j], n=16)
                    if dc <= 1:
                        k.wait("vector", s_out, s_out.v)
                    k.wait("vector", s_pd)
                    k.wait("vector", s_epin2[j])
                    k.do("vector", lambda e, j=j, db=db: e.tensor_add(
                        out=EPI[:, j], in0=ps_bank(k, db, 2), in1=EPI[:, j]))
                    k.do("vector", lambda e, j=j, sc=scale: e.scalar_tensor_tensor(
                        out=SGF[:, j], in0=EPI[:, j], scalar=sc, in1=XFF[:, j],
                        op0=ALU.mult, op1=ALU.add), inc=s_ep)
                    k.buf_release("XFF", "ffn_ep")
                    k.wait("sync", s_ep)
                    k.do("sync", lambda e, dc=dc, j=j, t0=t0: e.dma_start(
                        out=x_out[dc * 128:(dc + 1) * 128, t0:t0 + TP],
                        in_=SGF[:, j]), inc=s_out, n=16)
    return s_out





def emit_mamba_pre(k, x_in, xln1_out, z_out, gb_t, ip_t):
    sb = k.sb
    s_gb = k.sem("mb_gb")
    s_zmm = k.sem("mb_zmm")
    s_zev = k.sem("mb_zev")
    s_out = k.sem("ffn_out")
    k.do("gpsimd", lambda e: e.dma_start(
        out=sb["GAMB"][:], in_=gb_t.rearrange("a p c -> p a c")), inc=s_gb, n=16)
    k.do("gpsimd", lambda e: e.dma_start(
        out=sb["IPT"][:].rearrange("p (a o) -> p a o", a=NKC),
        in_=ip_t.rearrange("a p o -> p a o")), inc=s_gb, n=16)
    for pp in range(NTP):
        t0 = pp * TP
        emit_ln(k, x_in, t0, f32_out=xln1_out, write_xln=False)
        k.wait("sync", s_out, s_out.v)
        if pp == 0:
            k.wait("vector", s_gb, 32)
        s_xln = emit_ln(k, xln1_out, t0, gamma_beta=sb["GAMB"])
        xln_done = s_xln.v
        k.wait("tensor", s_gb, 32)
        k.wait("tensor", s_xln, xln_done)
        k.wait("tensor", s_zev, s_zev.v)
        for kc in range(NKC):
            for nck in range(TP // 512):
                k.do("tensor", lambda e, kc=kc, nck=nck: e.matmul(
                    out=k.sb["PS"][0:96, 4 + nck, :],
                    lhsT=sb["IPT"][:, kc * 96:(kc + 1) * 96],
                    rhs=sb["XLN"][:, kc, nck * 512:(nck + 1) * 512],
                    start=(kc == 0), stop=(kc == NKC - 1)),
                    inc=(s_zmm if kc == NKC - 1 and nck == TP // 512 - 1
                         else None))
        k.wait("vector", s_zmm)
        k.wait("vector", s_out, s_out.v)
        k.do("vector", lambda e: e.tensor_copy(
            out=sb["SGF"][0:96, 0, :],
            in_=k.sb["PS"][0:96, 4:6, :].rearrange("p a b -> p (a b)")),
            inc=s_zev)
        k.wait("sync", s_zev)
        k.do("sync", lambda e, t0=t0: e.dma_start(
            out=z_out[:, t0:t0 + TP], in_=sb["SGF"][0:96, 0, :]),
            inc=s_out, n=16)
    return s_out


def emit_mamba_mid(k, zf, cw3, xpt, dtt, dtb, alog, y_out, u_out,
                   d_dl, d_du, d_bc, d_a2):
    sb = k.sb
    nc = k.nc
    LSEQ = 2 * TL
    BUFA, BUFB = sb["BUFA"], sb["BUFB"]
    U, UB, DBC = sb["MU"], sb["MUB"], sb["MDBC"]
    s_ld = k.sem("mm_ld")
    s_cv = k.sem("mm_cv")
    s_u = k.sem("mm_u")
    s_dbc = k.sem("mm_dbc")
    s_dbce = k.sem("mm_dbce")
    s_drb = k.sem("mm_drb")
    s_dp = k.sem("mm_dp")
    s_dl = k.sem("mm_dl")
    s_scr = k.sem("mm_scr")
    s_a2 = k.sem("mm_a2")
    s_til = k.sem("mm_til")
    s_exo = k.sem("mm_exo")
    s_da = k.sem("mm_da")
    s_st = k.sem("mm_st")
    s_pb = k.sem("mm_pb")
    s_ymm = k.sem("mm_ymm")
    s_yev = k.sem("mm_yev")
    s_out = k.sem("ffn_out")
    k.do("gpsimd", lambda e: e.memset(BUFA[0:96, 0:1], 0.0), inc=s_ld)
    k.do("gpsimd", lambda e: e.memset(BUFA[0:96, LSEQ + 1:LSEQ + 2], 0.0),
         inc=s_ld)
    k.do("gpsimd", lambda e: e.dma_start(
        out=sb["MC3"][0:96, :], in_=cw3), inc=s_ld, n=16)
    k.do("gpsimd", lambda e: e.dma_start(
        out=sb["MCB"][0:96, :], in_=dtb), inc=s_ld, n=16)
    k.do("gpsimd", lambda e: e.dma_start(
        out=sb["MXPT"][0:96, :], in_=xpt), inc=s_ld, n=16)
    k.do("gpsimd", lambda e: e.dma_start(
        out=sb["MDTT"][0:6, :], in_=dtt), inc=s_ld, n=16)
    k.do("gpsimd", lambda e: e.dma_start(
        out=sb["MAL"][0:96, :], in_=alog), inc=s_ld, n=16)
    k.do("sync", lambda e: e.dma_start(
        out=BUFA[0:96, 1:LSEQ + 1], in_=zf), inc=s_ld, n=16)
    k.wait("vector", s_ld, s_ld.v)
    k.do("vector", lambda e: e.tensor_scalar(
        out=BUFB[0:96, 0:LSEQ], in0=BUFA[0:96, 0:LSEQ],
        scalar1=sb["MC3"][0:96, 0:1], scalar2=sb["MCB"][0:96, 2:3],
        op0=ALU.mult, op1=ALU.add))
    k.do("vector", lambda e: e.scalar_tensor_tensor(
        out=BUFB[0:96, 0:LSEQ], in0=BUFA[0:96, 1:LSEQ + 1],
        scalar=sb["MC3"][0:96, 1:2], in1=BUFB[0:96, 0:LSEQ],
        op0=ALU.mult, op1=ALU.add))
    k.do("vector", lambda e: e.scalar_tensor_tensor(
        out=BUFB[0:96, 0:LSEQ], in0=BUFA[0:96, 2:LSEQ + 2],
        scalar=sb["MC3"][0:96, 2:3], in1=BUFB[0:96, 0:LSEQ],
        op0=ALU.mult, op1=ALU.add), inc=s_cv)
    k.wait("scalar", s_cv)
    k.do("scalar", lambda e: e.activation(
        out=U[0:96, :], in_=BUFB[0:96, 0:LSEQ], func=AF.Silu))
    k.do("scalar", lambda e: e.copy(out=UB[0:96, :], in_=U[0:96, :]),
         inc=s_u)
    k.wait("sync", s_u)
    k.do("sync", lambda e: e.dma_start(out=u_out, in_=U[0:96, :]),
         inc=s_out, n=16)
    k.wait("scalar", s_ld, s_ld.v)
    k.do("scalar", lambda e: e.activation(
        out=sb["MA2"][0:96, :], in_=sb["MAL"][0:96, :], func=AF.Exp),
        inc=s_a2)
    k.wait("vector", s_a2)
    k.do("vector", lambda e: e.tensor_scalar_mul(
        out=sb["MA2"][0:96, :], in0=sb["MA2"][0:96, :], scalar1=-0.001),
        inc=s_a2)
    k.wait("gpsimd", s_a2, s_a2.v)
    k.do("gpsimd", lambda e: e.dma_start(out=d_a2, in_=sb["MA2"][0:96, :]),
         inc=s_scr, n=16)
    k.wait("tensor", s_u)
    for nck in range(LSEQ // 512):
        b = nck % 2
        k.wait("tensor", s_dbce, s_dbce.v - 1)
        k.do("tensor", lambda e, nck=nck, b=b: e.matmul(
            out=k.sb["PS"][0:38, 4 + b, :], lhsT=sb["MXPT"][0:96, :],
            rhs=UB[0:96, nck * 512:(nck + 1) * 512],
            start=True, stop=True), inc=s_dbc)
        k.wait("vector", s_dbc)
        k.do("vector", lambda e, nck=nck, b=b: e.tensor_copy(
            out=DBC[0:38, nck * 512:(nck + 1) * 512],
            in_=k.sb["PS"][0:38, 4 + b, :]), inc=s_dbce)
    k.do("vector", lambda e: e.tensor_copy(
        out=sb["MDRB"][0:6, :], in_=DBC[32:38, :]), inc=s_drb)
    k.wait("gpsimd", s_drb, s_drb.v)
    k.do("gpsimd", lambda e: e.dma_start(out=d_bc, in_=DBC[0:32, :]),
         inc=s_scr, n=16)
    k.wait("tensor", s_drb, s_drb.v)
    for nck in range(LSEQ // 512):
        b = nck % 2
        k.wait("tensor", s_dl, s_dl.v - 1)
        k.do("tensor", lambda e, nck=nck, b=b: e.matmul(
            out=k.sb["PS"][0:96, 6 + b, :], lhsT=sb["MDTT"][0:6, :],
            rhs=sb["MDRB"][0:6, nck * 512:(nck + 1) * 512],
            start=True, stop=True), inc=s_dp)
        k.wait("scalar", s_dp)
        k.do("scalar", lambda e, nck=nck, b=b: e.activation(
            out=BUFA[0:96, nck * 512:(nck + 1) * 512],
            in_=k.sb["PS"][0:96, 6 + b, :], func=AF.Abs,
            bias=sb["MCB"][0:96, 0:1], scale=1.0))
        k.do("scalar", lambda e, nck=nck, b=b: e.activation(
            out=BUFB[0:96, nck * 512:(nck + 1) * 512],
            in_=k.sb["PS"][0:96, 6 + b, :], func=AF.Relu,
            bias=sb["MCB"][0:96, 0:1], scale=1.0), inc=s_dl)
    k.wait("scalar", s_dl, s_dl.v)
    k.do("scalar", lambda e: e.activation(
        out=BUFA[0:96, 0:LSEQ], in_=BUFA[0:96, 0:LSEQ], func=AF.Exp,
        scale=-1.0))
    k.do("scalar", lambda e: e.activation(
        out=BUFA[0:96, 0:LSEQ], in_=BUFA[0:96, 0:LSEQ], func=AF.Ln,
        bias=sb["CF"][0:96, 1:2], scale=1.0), inc=s_dl)
    k.wait("vector", s_dl, s_dl.v)
    k.do("vector", lambda e: e.tensor_add(
        out=BUFA[0:96, 0:LSEQ], in0=BUFA[0:96, 0:LSEQ],
        in1=BUFB[0:96, 0:LSEQ]))
    k.do("vector", lambda e: e.scalar_tensor_tensor(
        out=BUFB[0:96, 0:LSEQ], in0=BUFA[0:96, 0:LSEQ], scalar=0.001,
        in1=U[0:96, :], op0=ALU.mult, op1=ALU.mult), inc=s_dl)
    k.wait("gpsimd", s_dl, s_dl.v)
    k.do("gpsimd", lambda e: e.dma_start(out=d_dl, in_=BUFA[0:96, 0:LSEQ]),
         inc=s_scr, n=16)
    k.do("gpsimd", lambda e: e.dma_start(out=d_du, in_=BUFB[0:96, 0:LSEQ]),
         inc=s_scr, n=16)
    DLR, DUR, BR, CR = sb["MDLR"], sb["MDUR"], sb["MBR"], sb["MCR"]
    scr_done = s_scr.v
    for ti in range(12):
        d0 = 8 * ti
        k.wait("sync", s_scr, scr_done)
        k.wait("sync", s_st, s_st.v)
        k.wait("sync", s_da, s_da.v)
        k.do("sync", lambda e, d0=d0: e.dma_start(
            out=DLR[:], in_=d_dl[d0:d0 + 8].unsqueeze(1).broadcast_to(
                [8, 16, LSEQ])), inc=s_til, n=16)
        k.do("sync", lambda e, d0=d0: e.dma_start(
            out=DUR[:], in_=d_du[d0:d0 + 8].unsqueeze(1).broadcast_to(
                [8, 16, LSEQ])), inc=s_til, n=16)
        k.do("sync", lambda e: e.dma_start(
            out=BR[:], in_=d_bc[0:16].unsqueeze(0).broadcast_to(
                [8, 16, LSEQ])), inc=s_til, n=16)
        k.do("sync", lambda e: e.dma_start(
            out=CR[:], in_=d_bc[16:32].unsqueeze(0).broadcast_to(
                [8, 16, LSEQ])), inc=s_til, n=16)
        k.do("sync", lambda e, d0=d0: e.dma_start(
            out=sb["MA2C"][:, 0:1],
            in_=d_a2[d0:d0 + 8].rearrange("d n -> (d n)").unsqueeze(1)),
            inc=s_til, n=16)
        k.wait("vector", s_til)
        k.do("vector", lambda e: e.tensor_scalar_mul(
            out=DLR[:], in0=DLR[:], scalar1=sb["MA2C"][:, 0:1]), inc=s_exo)
        k.wait("scalar", s_exo)
        k.do("scalar", lambda e: e.activation(
            out=sb["MDA"][:], in_=DLR[:], func=AF.Exp), inc=s_da)
        k.wait("vector", s_da)
        k.do("vector", lambda e: e.tensor_mul(
            out=DUR[:], in0=DUR[:], in1=BR[:]))
        k.do("vector", lambda e: e.tensor_tensor_scan(
            out=sb["MST"][:], data0=sb["MDA"][:], data1=DUR[:],
            initial=0.0, op0=ALU.mult, op1=ALU.add))
        k.do("vector", lambda e: e.tensor_mul(
            out=CR[:], in0=sb["MST"][:], in1=CR[:]), inc=s_st)
        k.wait("scalar", s_st)
        k.wait("scalar", s_ymm, s_ymm.v)
        k.do("scalar", lambda e: e.copy(out=sb["MPB"][:], in_=CR[:]),
             inc=s_pb)
        k.wait("tensor", s_pb)
        for nck in range(LSEQ // 512):
            b = nck % 2
            k.wait("tensor", s_yev, s_yev.v - 1)
            k.do("tensor", lambda e, nck=nck, b=b: e.matmul(
                out=k.sb["PS"][0:8, 4 + b, :], lhsT=sb["MSEL"][:],
                rhs=sb["MPB"][:, nck * 512:(nck + 1) * 512],
                start=True, stop=True), inc=s_ymm)
            if nck == 0:
                k.wait("vector", s_out, s_out.v)
            k.wait("vector", s_ymm)
            k.do("vector", lambda e, nck=nck, b=b: e.tensor_copy(
                out=sb["MYH"][0:8, nck * 512:(nck + 1) * 512],
                in_=k.sb["PS"][0:8, 4 + b, :]), inc=s_yev)
        k.wait("sync", s_yev, s_yev.v)
        k.do("sync", lambda e, d0=d0: e.dma_start(
            out=y_out[d0:d0 + 8, :], in_=sb["MYH"][0:8, :]), inc=s_out, n=16)
    return s_out


def emit_mamba_post(k, y_in, u_in, dpar, xres, xln1, x_out, op_t):
    sb = k.sb
    s_ld = k.sem("mp_ld")
    s_yb = k.sem("mp_yb")
    s_mm = k.sem("mp_mm")
    s_ein2 = [k.sem("mp_ein0"), k.sem("mp_ein1")]
    s_ep = k.sem("mp_ep")
    s_out = k.sem("ffn_out")
    k.do("gpsimd", lambda e: e.dma_start(
        out=sb["MOPT"][0:96, :].rearrange("p (a o) -> p a o", a=NKC),
        in_=op_t.rearrange("a p o -> p a o")), inc=s_ld, n=16)
    k.do("gpsimd", lambda e: e.dma_start(out=sb["MC3A"][0:96, 0:1], in_=dpar),
         inc=s_ld, n=16)
    k.do("sync", lambda e: e.dma_start(out=sb["MYA"][0:96, :], in_=y_in),
         inc=s_ld, n=16)
    k.do("sync", lambda e: e.dma_start(out=sb["MUA"][0:96, :], in_=u_in),
         inc=s_ld, n=16)
    k.wait("vector", s_ld, s_ld.v)
    k.do("vector", lambda e: e.scalar_tensor_tensor(
        out=sb["MYA"][0:96, :], in0=sb["MUA"][0:96, :],
        scalar=sb["MC3A"][0:96, 0:1], in1=sb["MYA"][0:96, :],
        op0=ALU.mult, op1=ALU.add), inc=s_yb)
    k.wait("scalar", s_yb)
    k.do("scalar", lambda e: e.copy(out=sb["MYB"][0:96, :],
                                    in_=sb["MYA"][0:96, :]), inc=s_yb)
    step = 0
    for oc in range(NKC):
        for th in range(NTP):
            j = step % 2
            gb = 4 if j == 0 else 6
            t0 = th * TP
            k.wait("tensor", s_yb, s_yb.v)
            k.wait("tensor", s_ep, s_ep.v - 1)
            for nck in range(TP // 512):
                k.do("tensor", lambda e, oc=oc, t0=t0, nck=nck, gb=gb: e.matmul(
                    out=ps_bank(k, gb + nck),
                    lhsT=sb["MOPT"][0:96, oc * 128:(oc + 1) * 128],
                    rhs=sb["MYB"][0:96, t0 + nck * 512:t0 + (nck + 1) * 512],
                    start=True, stop=True),
                    inc=(s_mm if nck == TP // 512 - 1 else None))
            k.wait("sync", s_ep, s_ep.v - 1)
            k.do("sync", lambda e, oc=oc, t0=t0, j=j: e.dma_start(
                out=sb["XFF"][:, j],
                in_=xres[oc * 128:(oc + 1) * 128, t0:t0 + TP]),
                inc=s_ein2[j], n=16)
            k.do("sync", lambda e, oc=oc, t0=t0, j=j: e.dma_start(
                out=sb["SGF"][:, j],
                in_=xln1[oc * 128:(oc + 1) * 128, t0:t0 + TP]),
                inc=s_ein2[j], n=16)
            k.wait("vector", s_mm)
            k.wait("vector", s_ein2[j])
            k.do("vector", lambda e, j=j, gb=gb: e.tensor_add(
                out=sb["XFF"][:, j], in0=ps_bank(k, gb, 2), in1=sb["XFF"][:, j]))
            k.do("vector", lambda e, j=j: e.tensor_add(
                out=sb["XFF"][:, j], in0=sb["XFF"][:, j], in1=sb["SGF"][:, j]),
                inc=s_ep)
            k.wait("sync", s_ep)
            k.do("sync", lambda e, oc=oc, t0=t0, j=j: e.dma_start(
                out=x_out[oc * 128:(oc + 1) * 128, t0:t0 + TP],
                in_=sb["XFF"][:, j]), inc=s_out, n=16)
            step += 1
    return s_out


def alloc_mamba_pre(k):
    nc = k.nc
    sb = k.sb
    sb["GAMB"] = nc.alloc_sbuf_tensor("GAMB", [128, NKC, 2], F32)
    sb["IPT"] = nc.alloc_sbuf_tensor("IPT", [128, NKC * 96], BF16)


def alloc_mamba_post(k):
    nc = k.nc
    sb = k.sb
    sb["MOPT"] = nc.alloc_sbuf_tensor("MOPT", [96, NKC * 128], BF16)
    sb["MC3A"] = nc.alloc_sbuf_tensor("MC3A", [96, 1], F32)
    sb["MYA"] = nc.alloc_sbuf_tensor("MYA", [96, TL], F32)
    sb["MUA"] = nc.alloc_sbuf_tensor("MUA", [96, TL], F32)
    sb["MYB"] = nc.alloc_sbuf_tensor("MYB", [96, TL], BF16)


def alloc_mamba_mid(k):
    nc = k.nc
    sb = k.sb
    LSEQ = 2 * TL
    sb["BUFA"] = nc.alloc_sbuf_tensor("BUFA", [128, LSEQ + 2], F32)
    sb["BUFB"] = nc.alloc_sbuf_tensor("BUFB", [128, LSEQ], F32)
    sb["MU"] = nc.alloc_sbuf_tensor("MU", [96, LSEQ], F32)
    sb["MUB"] = nc.alloc_sbuf_tensor("MUB", [96, LSEQ], BF16)
    sb["MDBC"] = nc.alloc_sbuf_tensor("MDBC", [38, LSEQ], F32)
    sb["MDRB"] = nc.alloc_sbuf_tensor("MDRB", [6, LSEQ], BF16)
    sb["MC3"] = nc.alloc_sbuf_tensor("MC3", [96, 3], F32)
    sb["MCB"] = nc.alloc_sbuf_tensor("MCB", [96, 3], F32)
    sb["MXPT"] = nc.alloc_sbuf_tensor("MXPT", [96, 38], BF16)
    sb["MDTT"] = nc.alloc_sbuf_tensor("MDTT", [6, 96], BF16)
    sb["MAL"] = nc.alloc_sbuf_tensor("MAL", [96, 16], F32)
    sb["MA2"] = nc.alloc_sbuf_tensor("MA2", [96, 16], F32)
    sb["MA2C"] = nc.alloc_sbuf_tensor("MA2C", [128, 1], F32)
    sb["MDLR"] = nc.alloc_sbuf_tensor("MDLR", [128, LSEQ], F32)
    sb["MDUR"] = nc.alloc_sbuf_tensor("MDUR", [128, LSEQ], F32)
    sb["MBR"] = nc.alloc_sbuf_tensor("MBR", [128, LSEQ], F32)
    sb["MCR"] = nc.alloc_sbuf_tensor("MCR", [128, LSEQ], F32)
    sb["MDA"] = nc.alloc_sbuf_tensor("MDA", [128, LSEQ], F32)
    sb["MST"] = nc.alloc_sbuf_tensor("MST", [128, LSEQ], F32)
    sb["MPB"] = nc.alloc_sbuf_tensor("MPB", [128, LSEQ], BF16)
    sb["MYH"] = nc.alloc_sbuf_tensor("MYH", [8, LSEQ], F32)
    sb["MSEL"] = nc.alloc_sbuf_tensor("MSEL", [128, 8], BF16)
    sb["CF"] = nc.alloc_sbuf_tensor("CF", [128, 4], F32)
    sb["PS"] = nc.alloc_psum_tensor("PS", [128, 8, 512], F32)
    s_init = k.sem("s_init")
    k.do("gpsimd", lambda e: e.memset(sb["CF"][:, 1:2], 1.0), inc=s_init)
    selv = np.zeros((128, 8), dtype=np.float32)
    for p in range(128):
        selv[p, p // 16] = 1.0
    sel = nc.inline_tensor(selv, "selv16")
    k.do("gpsimd", lambda e, sel=sel: e.dma_start(
        out=sb["MSEL"][:], in_=sel.ap()), inc=s_init, n=16)





def emit_attn_ln(k, x_in, xln_out):
    s_out = k.sem("ffn_out")
    for pp in range(NTP):
        t0 = pp * TP
        s_xln = emit_ln(k, x_in, t0)
        for kc in range(NKC):
            k.wait("sync", s_xln, s_xln.v - (NKC - 1 - kc))
            k.do("sync", lambda e, kc=kc, t0=t0: e.dma_start(
                out=xln_out[kc * 128:(kc + 1) * 128, t0:t0 + TP],
                in_=k.sb["XLN"][:, kc]), inc=s_out, n=16)
    return s_out


def emit_attn_core(k, xf, wqkv, v_out, nh=8):
    sb = k.sb
    LSEQ = 2 * TL
    s_x = k.sem("at_x")
    s_w = [k.sem("at_w0"), k.sem("at_w1")]
    s_qkv = k.sem("at_qkv")
    s_ev = k.sem("at_ev")
    s_sc = k.sem("at_sc")
    s_et = k.sem("at_et")
    s_z = k.sem("at_z")
    s_r = k.sem("at_r")
    s_acc = k.sem("at_acc")
    s_ab = k.sem("at_ab")
    s_cs = k.sem("at_cs")
    s_cse = k.sem("at_cse")
    s_cb = k.sem("at_cb")
    s_vv = k.sem("at_vv")
    s_out = k.sem("ffn_out")
    s_init = k.sem("s_init")
    XB, QH, KH, VH = sb["AXB"], sb["AQ"], sb["AK"], sb["AV"]
    EB, ACC, ZR = sb["AEB"], sb["AACC"], sb["AZR"]
    CS = ACC[0:1, :]
    WB = sb["AWB"]
    PS = sb["PS"]
    for kc in range(NKC):
        k.do("sync", lambda e, kc=kc: e.dma_start(
            out=XB[:, kc * LSEQ:(kc + 1) * LSEQ],
            in_=xf[kc * 128:(kc + 1) * 128, :]), inc=s_x, n=16)
    for hh in range(nh):
        for which in range(3):
            oc = hh * 3 + which
            j = oc % 2
            k.buf_guard("AWB", "gpsimd")
            k.do("gpsimd", lambda e, oc=oc, j=j: e.dma_start(
                out=WB[:, j], in_=wqkv[oc]), inc=s_w[j], n=16)
            k.wait("tensor", s_w[j])
            k.wait("tensor", s_x, 16 * NKC)
            if hh == 0 and which == 0:
                k.wait("tensor", s_init, k.sem("s_init").v)
            dst = (QH, KH, VH)[which]
            for nck in range(LSEQ // 512):
                b = nck % 2
                k.wait("tensor", s_ev, s_ev.v - 1)
                for kc in range(NKC):
                    k.do("tensor", lambda e, j=j, kc=kc, nck=nck, b=b: e.matmul(
                        out=ps_bank(k, b),
                        lhsT=WB[:, j, kc * 128:(kc + 1) * 128],
                        rhs=XB[:, kc * LSEQ + nck * 512:
                               kc * LSEQ + (nck + 1) * 512],
                        start=(kc == 0), stop=(kc == NKC - 1)),
                        inc=(s_qkv if kc == NKC - 1 else None))
                k.wait("scalar", s_qkv)
                k.do("scalar", lambda e, dst=dst, nck=nck, b=b: e.copy(
                    out=dst[:, nck * 512:(nck + 1) * 512],
                    in_=ps_bank(k, b)), inc=s_ev)
            k.buf_release("AWB", "at_qkv")
        qkv_done = s_ev.v
        k.wait("gpsimd", s_ev, qkv_done)
        k.do("gpsimd", lambda e: e.memset(ACC[:], 0.0), inc=s_acc)
        for qt in range(32):
            for half in range(2):
                hb = 4 * half
                k.wait("tensor", s_ev, qkv_done)
                k.wait("tensor", s_et, s_et.v - 1)
                for ck in range(4):
                    kk = half * 2048 + ck * 512
                    k.do("tensor", lambda e, qt=qt, kk=kk, hb=hb, ck=ck: e.matmul(
                        out=ps_bank(k, hb + ck),
                        lhsT=QH[:, qt * 128:(qt + 1) * 128],
                        rhs=KH[:, kk:kk + 512],
                        start=True, stop=True),
                        inc=(s_sc if ck == 3 else None))
                k.wait("scalar", s_sc)
                if half == 0:
                    k.wait("scalar", s_z, s_z.v)
                    k.wait("scalar", s_acc, s_acc.v)
                k.do("scalar", lambda e, qt=qt, half=half, hb=hb: e.activation(
                    out=EB[:, half * 2048:(half + 1) * 2048],
                    in_=ps_bank(k, hb, 4), func=AF.Exp,
                    scale=float(1.0 / np.sqrt(128.0)),
                    accum_out=ZR[:, half:half + 1]), inc=s_et)
            k.wait("vector", s_et)
            k.wait("vector", s_acc)
            k.do("vector", lambda e: e.tensor_add(
                out=ZR[:, 2:3], in0=ZR[:, 0:1], in1=ZR[:, 1:2]), inc=s_z)
            k.wait("vector", s_z)
            k.do("vector", lambda e: e.reciprocal(out=ZR[:, 3:4],
                                                  in_=ZR[:, 2:3]), inc=s_r)
            k.wait("vector", s_r)
            k.do("vector", lambda e, qt=qt: e.scalar_tensor_tensor(
                out=ACC[:], in0=EB[:], scalar=ZR[:, 3:4], in1=ACC[:],
                op0=ALU.mult, op1=ALU.add), inc=s_acc)
        k.wait("scalar", s_acc)
        k.do("scalar", lambda e: e.copy(out=EB[:], in_=ACC[:]), inc=s_ab)
        for ck in range(8):
            k.wait("tensor", s_ab, s_ab.v)
            k.wait("tensor", s_cse, s_cse.v)
            k.do("tensor", lambda e, ck=ck: e.matmul(
                out=PS[0:1, 0, :], lhsT=sb["ONESB"][:],
                rhs=EB[:, ck * 512:(ck + 1) * 512],
                start=True, stop=True), inc=s_cs)
            k.wait("vector", s_cs)
            k.do("vector", lambda e, ck=ck: e.tensor_copy(
                out=CS[:, ck * 512:(ck + 1) * 512], in_=PS[0:1, 0, :]),
                inc=s_cse)
        for ck in range(8):
            b = 1 + ck % 2
            k.wait("tensor", s_cse, s_cse.v)
            k.wait("tensor", s_vv, s_vv.v - 1)
            k.do("tensor", lambda e, ck=ck, b=b: e.matmul(
                out=ps_bank(k, b), lhsT=sb["ONESRF"][:],
                rhs=CS[:, ck * 512:(ck + 1) * 512],
                start=True, stop=True), inc=s_cb)
            k.wait("vector", s_cb)
            k.do("vector", lambda e, ck=ck, b=b: e.tensor_mul(
                out=sb["AVAL"][:, ck * 512:(ck + 1) * 512],
                in0=VH[:, ck * 512:(ck + 1) * 512], in1=ps_bank(k, b)),
                inc=s_vv)
        k.wait("sync", s_vv, s_vv.v)
        k.do("sync", lambda e, hh=hh: e.dma_start(
            out=v_out[hh * 128:(hh + 1) * 128, :], in_=sb["AVAL"][:]),
            inc=s_out, n=16)
        k.wait("tensor", s_vv, s_vv.v)
    return s_out


def emit_attn_post(k, v_in, xres, x_out, wo_t):
    sb = k.sb
    s_v = k.sem("ap_v")
    s_w = [k.sem("ap_w0"), k.sem("ap_w1")]
    s_ein2 = [k.sem("ap_ein0"), k.sem("ap_ein1")]
    s_mm = k.sem("ap_mm")
    s_ein = k.sem("ap_ein")
    s_ep = k.sem("ap_ep")
    s_out = k.sem("ffn_out")
    VB = sb["XLN"]
    WB = sb["WB"]
    for pp in range(NTP):
        t0 = pp * TP
        k.wait("sync", s_ep, s_ep.v)
        k.do("sync", lambda e, t0=t0: e.dma_start(
            out=VB[:].rearrange("p a t -> p a t"),
            in_=v_in.rearrange("(a p) t -> p a t", p=128)[:, :, t0:t0 + TP]),
            inc=s_v, n=16)
        for oc in range(NKC):
            j = oc % 2
            gb = 4 if j == 0 else 0
            k.buf_guard("WB", "gpsimd")
            k.do("gpsimd", lambda e, oc=oc, j=j: e.dma_start(
                out=WB[:, j, 0:2048], in_=wo_t[oc]), inc=s_w[j], n=16)
            k.wait("tensor", s_w[j])
            k.wait("tensor", s_v, s_v.v)
            k.wait("tensor", s_ep, s_ep.v - 1)
            for kc in range(NKC):
                for nck in range(TP // 512):
                    k.do("tensor", lambda e, j=j, kc=kc, nck=nck, gb=gb: e.matmul(
                        out=ps_bank(k, gb + nck),
                        lhsT=WB[:, j, kc * 128:(kc + 1) * 128],
                        rhs=VB[:, kc, nck * 512:(nck + 1) * 512],
                        start=(kc == 0), stop=(kc == NKC - 1)),
                        inc=(s_mm if kc == NKC - 1 and nck == TP // 512 - 1
                             else None))
            k.buf_release("WB", "ap_mm")
            k.buf_guard("XFF", "sync")
            k.do("sync", lambda e, oc=oc, t0=t0, j=j: e.dma_start(
                out=sb["XFF"][:, j],
                in_=xres[oc * 128:(oc + 1) * 128, t0:t0 + TP]), inc=s_ein2[j], n=16)
            k.wait("vector", s_mm)
            k.wait("vector", s_ein2[j])
            k.do("vector", lambda e, j=j, gb=gb: e.tensor_add(
                out=sb["XFF"][:, j], in0=ps_bank(k, gb, 2), in1=sb["XFF"][:, j]),
                inc=s_ep)
            k.buf_release("XFF", "ap_ep")
            k.wait("sync", s_ep)
            k.do("sync", lambda e, oc=oc, t0=t0, j=j: e.dma_start(
                out=x_out[oc * 128:(oc + 1) * 128, t0:t0 + TP],
                in_=sb["XFF"][:, j]), inc=s_out, n=16)
    return s_out


def alloc_attn_core(k):
    nc = k.nc
    sb = k.sb
    LSEQ = 2 * TL
    sb["AXB"] = nc.alloc_sbuf_tensor("AXB", [128, NKC * LSEQ], BF16)
    sb["AQ"] = nc.alloc_sbuf_tensor("AQ", [128, LSEQ], BF16)
    sb["AK"] = nc.alloc_sbuf_tensor("AK", [128, LSEQ], BF16)
    sb["AV"] = nc.alloc_sbuf_tensor("AV", [128, LSEQ], BF16)
    sb["AEB"] = nc.alloc_sbuf_tensor("AEB", [128, LSEQ], BF16)
    sb["AACC"] = nc.alloc_sbuf_tensor("AACC", [128, LSEQ], F32)
    sb["AZR"] = nc.alloc_sbuf_tensor("AZR", [128, 8], F32)
    sb["AVAL"] = nc.alloc_sbuf_tensor("AVAL", [128, LSEQ], BF16)
    sb["AWB"] = nc.alloc_sbuf_tensor("AWB", [128, 2, 2048], BF16)
    sb["ONESB"] = nc.alloc_sbuf_tensor("ONESB", [128, 1], BF16)
    sb["ONESRF"] = nc.alloc_sbuf_tensor("ONESRF", [1, 128], F32)
    sb["PS"] = nc.alloc_psum_tensor("PS", [128, 8, 512], F32)
    s_init = k.sem("s_init")
    k.do("gpsimd", lambda e: e.memset(sb["ONESB"][:], 1.0), inc=s_init)
    k.do("gpsimd", lambda e: e.memset(sb["ONESRF"][:], 1.0), inc=s_init)


def build_attn_ln_program():
    k = K()
    nc = k.nc
    xin = nc.dram_tensor("xin", [D, TL], F32, kind="ExternalInput").ap()
    xlnout = nc.dram_tensor("xlnout", [D, TL], BF16, kind="ExternalOutput").ap()
    alloc_common(k)
    s_out = emit_attn_ln(k, xin, xlnout)
    k.wait("gpsimd", s_out)
    return k.finalize()


def build_attn_core_program():
    k = K()
    nc = k.nc
    LSEQ = 2 * TL
    xf = nc.dram_tensor("xf", [D, LSEQ], BF16, kind="ExternalInput").ap()
    wqkv = nc.dram_tensor("wqkv", [24, 128, 2048], F32, kind="ExternalInput").ap()
    vout = nc.dram_tensor("vout", [1024, LSEQ], BF16, kind="ExternalOutput").ap()
    alloc_attn_core(k)
    s_out = emit_attn_core(k, xf, wqkv, vout)
    k.wait("gpsimd", s_out)
    return k.finalize()


def build_attn_post_program():
    k = K()
    nc = k.nc
    vin = nc.dram_tensor("vin", [D, TL], BF16, kind="ExternalInput").ap()
    xres = nc.dram_tensor("xres", [D, TL], F32, kind="ExternalInput").ap()
    wo = nc.dram_tensor("wo", [NKC, 128, 2048], F32, kind="ExternalInput").ap()
    xout = nc.dram_tensor("xout", [D, TL], F32, kind="ExternalOutput").ap()
    alloc_common(k)
    s_out = emit_attn_post(k, vin, xres, xout, wo)
    k.wait("gpsimd", s_out)
    return k.finalize()


def run_attn(x_full, w_qkv, w_o):
    import ml_dtypes
    xs = shard_x(x_full)
    ncA = get_prog("attn_ln", build_attn_ln_program)
    res = run_bass_kernel_spmd(ncA, [{"xin": xs[c]} for c in range(NCC)],
                               core_ids=list(range(NCC)))
    xln = [np.asarray(res.results[c]["xlnout"]) for c in range(NCC)]
    xfull = [np.ascontiguousarray(np.concatenate(
        [xln[2 * b], xln[2 * b + 1]], axis=1)) for b in range(4)]
    wq = np.asarray(w_qkv, np.float32)
    wq4 = wq.reshape(NKC, 128, 3, 16, 128)
    in_maps = []
    wqkv_cache = {}
    for c in range(NCC):
        b, hg = c // 2, c % 2
        if hg not in wqkv_cache:
            cols = []
            for hh in range(8):
                head = hg * 8 + hh
                for which in range(3):
                    cols.append(wq4[:, :, which, head, :])
            wt = np.stack(cols, 0)
            wqkv_cache[hg] = np.ascontiguousarray(
                wt.transpose(0, 2, 1, 3).reshape(24, 128, 2048))
        in_maps.append({"xf": xfull[b], "wqkv": wqkv_cache[hg]})
    ncB = get_prog("attn_core", build_attn_core_program)
    res = run_bass_kernel_spmd(ncB, in_maps, core_ids=list(range(NCC)))
    vout = [np.asarray(res.results[c]["vout"]) for c in range(NCC)]
    wo_t = np.ascontiguousarray(
        np.asarray(w_o, np.float32).reshape(NKC, 128, NKC, 128)
        .transpose(2, 1, 0, 3).reshape(NKC, 128, 2048))
    in_maps = []
    for c in range(NCC):
        b, h = c // 2, c % 2
        vtok = np.concatenate([vout[2 * b], vout[2 * b + 1]], axis=0)
        vtok = np.ascontiguousarray(vtok[:, h * TL:(h + 1) * TL])
        in_maps.append({"vin": vtok, "xres": xs[c], "wo": wo_t})
    ncC = get_prog("attn_post", build_attn_post_program)
    res = run_bass_kernel_spmd(ncC, in_maps, core_ids=list(range(NCC)))
    return unshard_x([np.asarray(res.results[c]["xout"]) for c in range(NCC)])



def ffn_perm():
    inew = np.arange(IF)
    return (inew % 2048) * 4 + inew // 2048


def prep_ffn_weights(w_gate, w_up, w_down):
    perm = ffn_perm()
    wg = np.asarray(w_gate)[:, perm]
    wu = np.asarray(w_up)[:, perm]
    wd = np.asarray(w_down)[perm, :]
    wg_t = np.ascontiguousarray(
        wg.reshape(NKC, 128, 64, 128).transpose(2, 1, 0, 3).reshape(64, 128, 2048))
    wu_t = np.ascontiguousarray(
        wu.reshape(NKC, 128, 64, 128).transpose(2, 1, 0, 3).reshape(64, 128, 2048))
    wd_rows = wd.reshape(64, 128, D)[FFN_CHUNK_ORDER]
    wd_t = np.ascontiguousarray(
        wd_rows.reshape(64, 128, NKC, 128).transpose(2, 1, 0, 3).reshape(NKC, 128, 8192))
    return wg_t.astype(np.float32), wu_t.astype(np.float32), wd_t.astype(np.float32)


def shard_x(x):
    out = []
    for c in range(NCC):
        b, h = c // 2, c % 2
        sl = np.asarray(x[b, h * TL:(h + 1) * TL, :]).T
        out.append(np.ascontiguousarray(sl.astype(np.float32)))
    return out


def unshard_x(slices):
    x = np.empty((4, 2 * TL, D), np.float32)
    for c in range(NCC):
        b, h = c // 2, c % 2
        x[b, h * TL:(h + 1) * TL, :] = slices[c].T
    return x



def build_mlp_program(kind):
    k = K()
    nc = k.nc
    xin = nc.dram_tensor("xin", [D, TL], F32, kind="ExternalInput").ap()
    wg = nc.dram_tensor("wg", [64, 128, 2048], F32, kind="ExternalInput").ap()
    wu = nc.dram_tensor("wu", [64, 128, 2048], F32, kind="ExternalInput").ap()
    wd = nc.dram_tensor("wd", [NKC, 128, 8192], F32, kind="ExternalInput").ap()
    rw = (nc.dram_tensor("rw", [NKC, 128, 8], F32, kind="ExternalInput").ap()
          if kind == "moe" else None)
    xout = nc.dram_tensor("xout", [D, TL], F32, kind="ExternalOutput").ap()
    part = nc.dram_tensor("part", [D, TP], F32, kind="Internal").ap()
    alloc_common(k)
    s_out = emit_mlp(k, xin, xout, wg, wu, wd, part, kind=kind, rw_t=rw)
    k.wait("gpsimd", s_out)
    return k.finalize()


def prep_moe_weights(router_w, gate_proj, up_proj, down_proj):
    rw_t = np.ascontiguousarray(
        np.asarray(router_w, np.float32).reshape(NKC, 128, 8))
    gp = np.asarray(gate_proj, np.float32).reshape(8, NKC, 128, 8, 128)
    wge_t = np.ascontiguousarray(
        gp.transpose(0, 3, 2, 1, 4).reshape(64, 128, 2048))
    up = np.asarray(up_proj, np.float32).reshape(8, NKC, 128, 8, 128)
    wue_t = np.ascontiguousarray(
        up.transpose(0, 3, 2, 1, 4).reshape(64, 128, 2048))
    dp = np.asarray(down_proj, np.float32).reshape(8, 8, 128, NKC, 128)
    wde_t = np.ascontiguousarray(
        dp.transpose(3, 2, 0, 1, 4).reshape(NKC, 128, 8192))
    return rw_t, wge_t, wue_t, wde_t


def build_mamba_pre_program():
    k = K()
    nc = k.nc
    xin = nc.dram_tensor("xin", [D, TL], F32, kind="ExternalInput").ap()
    gb = nc.dram_tensor("gb", [NKC, 128, 2], F32, kind="ExternalInput").ap()
    ipt = nc.dram_tensor("ipt", [NKC, 128, 96], F32, kind="ExternalInput").ap()
    xln1 = nc.dram_tensor("xln1", [D, TL], F32, kind="ExternalOutput").ap()
    z = nc.dram_tensor("z", [96, TL], F32, kind="ExternalOutput").ap()
    alloc_common(k)
    alloc_mamba_pre(k)
    s_out = emit_mamba_pre(k, xin, xln1, z, gb, ipt)
    k.wait("gpsimd", s_out)
    return k.finalize()


def build_mamba_mid_program():
    k = K()
    nc = k.nc
    LSEQ = 2 * TL
    zf = nc.dram_tensor("zf", [96, LSEQ], F32, kind="ExternalInput").ap()
    cw3 = nc.dram_tensor("cw3", [96, 3], F32, kind="ExternalInput").ap()
    cbdt = nc.dram_tensor("cbdt", [96, 3], F32, kind="ExternalInput").ap()
    xpt = nc.dram_tensor("xpt", [96, 38], F32, kind="ExternalInput").ap()
    dtt = nc.dram_tensor("dtt", [6, 96], F32, kind="ExternalInput").ap()
    alog = nc.dram_tensor("alog", [96, 16], F32, kind="ExternalInput").ap()
    y = nc.dram_tensor("y", [96, LSEQ], F32, kind="ExternalOutput").ap()
    u = nc.dram_tensor("u", [96, LSEQ], F32, kind="ExternalOutput").ap()
    d_dl = nc.dram_tensor("d_dl", [96, LSEQ], F32, kind="Internal").ap()
    d_du = nc.dram_tensor("d_du", [96, LSEQ], F32, kind="Internal").ap()
    d_bc = nc.dram_tensor("d_bc", [32, LSEQ], F32, kind="Internal").ap()
    d_a2 = nc.dram_tensor("d_a2", [96, 16], F32, kind="Internal").ap()
    alloc_mamba_mid(k)
    s_out = emit_mamba_mid(k, zf, cw3, xpt, dtt, cbdt, alog, y, u,
                           d_dl, d_du, d_bc, d_a2)
    k.wait("gpsimd", s_out)
    return k.finalize()


def build_mamba_post_program():
    k = K()
    nc = k.nc
    y = nc.dram_tensor("y", [96, TL], F32, kind="ExternalInput").ap()
    u = nc.dram_tensor("u", [96, TL], F32, kind="ExternalInput").ap()
    dpar = nc.dram_tensor("dpar", [96, 1], F32, kind="ExternalInput").ap()
    xres = nc.dram_tensor("xres", [D, TL], F32, kind="ExternalInput").ap()
    xln1 = nc.dram_tensor("xln1", [D, TL], F32, kind="ExternalInput").ap()
    opt = nc.dram_tensor("opt", [NKC, 96, 128], F32, kind="ExternalInput").ap()
    xout = nc.dram_tensor("xout", [D, TL], F32, kind="ExternalOutput").ap()
    sb = k.sb
    sb["XFF"] = nc.alloc_sbuf_tensor("XFF", [128, 2, TP], F32)
    sb["SGF"] = nc.alloc_sbuf_tensor("SGF", [128, 2, TP], F32)
    sb["PS"] = nc.alloc_psum_tensor("PS", [128, 8, 512], F32)
    alloc_mamba_post(k)
    s_out = emit_mamba_post(k, y, u, dpar, xres, xln1, xout, opt)
    k.wait("gpsimd", s_out)
    return k.finalize()


def run_mamba(x_full, ln_w, ln_b, in_proj_W, conv_w, conv_b, x_proj_W,
              dt_proj_W, dt_proj_b, A_log, D_param, out_proj_W):
    xs = shard_x(x_full)
    gb_t = np.ascontiguousarray(np.stack(
        [np.asarray(ln_w, np.float32).reshape(NKC, 128),
         np.asarray(ln_b, np.float32).reshape(NKC, 128)], axis=2))
    ip_t = np.ascontiguousarray(
        np.asarray(in_proj_W, np.float32).T.reshape(NKC, 128, 96))
    ncA = get_prog("mamba_pre", build_mamba_pre_program)
    res = run_bass_kernel_spmd(
        ncA, [{"xin": xs[c], "gb": gb_t, "ipt": ip_t} for c in range(NCC)],
        core_ids=list(range(NCC)))
    xln1 = [res.results[c]["xln1"] for c in range(NCC)]
    z = [res.results[c]["z"] for c in range(NCC)]
    zf = [np.ascontiguousarray(np.concatenate([z[2 * b], z[2 * b + 1]], axis=1))
          for b in range(4)]
    cw3 = np.ascontiguousarray(np.asarray(conv_w, np.float32).reshape(96, 3))
    cbdt = np.ascontiguousarray(np.stack(
        [np.asarray(dt_proj_b, np.float32), np.zeros(96, np.float32),
         np.asarray(conv_b, np.float32)], axis=1))
    xp = np.asarray(x_proj_W, np.float32)
    xp = np.concatenate([xp[6:22], xp[22:38], xp[0:6]], axis=0)
    xpt = np.ascontiguousarray(xp.T)
    dtt = np.ascontiguousarray(np.asarray(dt_proj_W, np.float32).T)
    alog = np.ascontiguousarray(np.asarray(A_log, np.float32))
    ncB = get_prog("mamba_mid", build_mamba_mid_program)
    res = run_bass_kernel_spmd(
        ncB, [{"zf": zf[c // 2], "cw3": cw3, "cbdt": cbdt, "xpt": xpt,
               "dtt": dtt, "alog": alog} for c in range(NCC)],
        core_ids=list(range(NCC)))
    ys = [res.results[c]["y"] for c in range(NCC)]
    us = [res.results[c]["u"] for c in range(NCC)]
    dpar = np.ascontiguousarray(np.asarray(D_param, np.float32).reshape(96, 1))
    opt = np.ascontiguousarray(
        np.asarray(out_proj_W, np.float32).reshape(NKC, 128, 96)
        .transpose(0, 2, 1))
    ncC = get_prog("mamba_post", build_mamba_post_program)
    in_maps = []
    for c in range(NCC):
        h = c % 2
        sl = slice(h * TL, (h + 1) * TL)
        in_maps.append({"y": np.ascontiguousarray(ys[c][:, sl]),
                        "u": np.ascontiguousarray(us[c][:, sl]),
                        "dpar": dpar, "xres": xs[c],
                        "xln1": xln1[c], "opt": opt})
    res = run_bass_kernel_spmd(ncC, in_maps, core_ids=list(range(NCC)))
    return unshard_x([res.results[c]["xout"] for c in range(NCC)])


_PROG_CACHE = {}


def get_prog(name, builder):
    if name not in _PROG_CACHE:
        _PROG_CACHE[name] = builder()
    return _PROG_CACHE[name]


def run_ffn(x_full, w_gate, w_up, w_down):
    wg_t, wu_t, wd_t = prep_ffn_weights(w_gate, w_up, w_down)
    xs = shard_x(x_full)
    nc = get_prog("ffn24", lambda: build_mlp_program("ffn24"))
    in_maps = [{"xin": xs[c], "wg": wg_t, "wu": wu_t, "wd": wd_t}
               for c in range(NCC)]
    res = run_bass_kernel_spmd(nc, in_maps, core_ids=list(range(NCC)))
    return unshard_x([res.results[c]["xout"] for c in range(NCC)])


def run_moe(x_full, router_w, gate_proj, up_proj, down_proj):
    rw_t, wge_t, wue_t, wde_t = prep_moe_weights(router_w, gate_proj,
                                                 up_proj, down_proj)
    xs = shard_x(x_full)
    nc = get_prog("moe", lambda: build_mlp_program("moe"))
    in_maps = [{"xin": xs[c], "wg": wge_t, "wu": wue_t, "wd": wde_t,
                "rw": rw_t} for c in range(NCC)]
    res = run_bass_kernel_spmd(nc, in_maps, core_ids=list(range(NCC)))
    return unshard_x([res.results[c]["xout"] for c in range(NCC)])


def kernel(**inputs):
    x = np.asarray(inputs["L0_attn_feat_patch"], np.float32)
    x = run_ffn(x, inputs["L0_ffn_w_gate"], inputs["L0_ffn_w_up"],
                inputs["L0_ffn_w_down"])
    x = run_mamba(x, inputs["L0_attn_ln_w"], inputs["L0_attn_ln_b"],
                  inputs["L0_attn_in_proj_W"], inputs["L0_attn_conv_w"],
                  inputs["L0_attn_conv_b"], inputs["L0_attn_x_proj_W"],
                  inputs["L0_attn_dt_proj_W"], inputs["L0_attn_dt_proj_b"],
                  inputs["L0_attn_A_log"], inputs["L0_attn_D_param"],
                  inputs["L0_attn_out_proj_W"])
    x = run_ffn(x, inputs["L0_ffn_w_gate"], inputs["L0_ffn_w_up"],
                inputs["L0_ffn_w_down"])
    x = run_moe(x, inputs["L1_ffn_router_w"], inputs["L1_ffn_gate_proj"],
                inputs["L1_ffn_up_proj"], inputs["L1_ffn_down_proj"])
    x = run_attn(x, inputs["L1_attn_w_qkv"], inputs["L1_attn_w_o"])
    x = run_moe(x, inputs["L1_ffn_router_w"], inputs["L1_ffn_gate_proj"],
                inputs["L1_ffn_up_proj"], inputs["L1_ffn_down_proj"])
    return x
